# version 07
# baseline (speedup 1.0000x reference)
import numpy as np
import ml_dtypes
import concourse.bass as bass
import concourse.mybir as mybir
from concourse.bass_utils import run_bass_kernel_spmd

F32 = mybir.dt.float32
BF16 = mybir.dt.bfloat16
U8 = mybir.dt.uint8
AF = mybir.ActivationFunctionType
ALU = mybir.AluOpType

B, S, D = 2, 4096, 2048
H, HKV, DH = 16, 4, 128
T = 1024
W = 4096
PRED = 3072
CH = 128
NCO = T // CH
NCP = PRED // CH
EPS = 1e-5

ENGS = ("tensor", "vector", "scalar", "gpsimd", "sync")


class Em:

    def __init__(self, nc):
        self.nc = nc
        self.q = {e: [] for e in ENGS}
        self.sems = {}
        self.cnt = {}
        self.hw = {e: {} for e in ENGS}

    def sem(self, name):
        self.sems[name] = self.nc.alloc_semaphore(name=name)
        self.cnt[name] = 0

    def emit(self, eng, fn, inc=None, amt=1):
        if inc is not None:
            self.cnt[inc] += amt
            h = self.sems[inc]

            def wrapped(e, fn=fn, h=h, amt=amt):
                fn(e).then_inc(h, amt)

            self.q[eng].append(wrapped)
            return self.cnt[inc]
        self.q[eng].append(lambda e, fn=fn: fn(e))
        return None

    def wait(self, eng, sem, val=None):
        v = self.cnt[sem] if val is None else val
        if v <= self.hw[eng].get(sem, 0):
            return
        self.hw[eng][sem] = v
        h = self.sems[sem]
        self.q[eng].append(lambda e, h=h, v=v: e.wait_ge(h, v))

    def waitev(self, eng, ev):
        if ev is not None:
            self.wait(eng, ev[0], ev[1])

    def c(self, sem):
        return self.cnt[sem]


def _hilo(x64):
    hi = np.asarray(x64, dtype=np.float32)
    lo = np.asarray(x64 - hi.astype(np.float64), dtype=np.float32)
    return hi, lo


def _lam():
    i = np.arange(1, H + 1, dtype=np.float64)
    return np.exp2(-8.0 * i / H)


def _tables():
    lam = _lam()
    r = np.arange(CH, dtype=np.float64)
    diff = r[None, :] - r[:, None]
    diagT = np.where(diff >= 0, np.exp(-lam[:, None, None] * diff[None]), 0.0)
    diagT = diagT.transpose(1, 0, 2).reshape(CH, H * CH)
    qd = np.exp(-lam[:, None] * (r[None, :] + 1.0)).reshape(1, H * CH)
    qdec = np.broadcast_to(qd, (128, H * CH)).copy()
    kdec = np.exp(-lam[None, :] * (CH - 1.0 - r[:, None]))
    blk = np.broadcast_to(np.exp(-lam * CH)[None, :], (128, H)).copy()
    c = np.arange(NCP, dtype=np.float64)
    kd2 = np.exp(-lam[:, None, None] * (PRED - 1.0 - CH * c[None, :, None] - r[None, None, :]))
    kd2 = kd2.transpose(2, 0, 1).reshape(CH, H * NCP)
    return {"diagT": _hilo(diagT), "qdec": _hilo(qdec), "kdec": _hilo(kdec),
            "blk": _hilo(blk), "kd2": _hilo(kd2)}


def _rope_tables(r):
    inv = 1.0 / (10000.0 ** (np.arange(0, DH, 2, dtype=np.float64) / DH))
    pos = np.arange(W, dtype=np.float64) + (r + 1) * T - W
    valid = pos >= 0
    posc = np.where(valid, pos, 0.0)
    fr = posc[None, :] * inv[:, None]
    cos = np.where(valid[None, :], np.cos(fr), 1.0)
    sin = np.where(valid[None, :], np.sin(fr), 0.0)
    cosT = np.concatenate([cos, cos], axis=0)
    sinT = np.concatenate([sin, sin], axis=0)
    return _hilo(cosT), _hilo(sinT)


STOP = None
SKIP_TRANS = False


def build_nc(taps=()):
    nc = bass.Bass("TRN2", target_bir_lowering=False, debug=False, detect_race_conditions=False)

    def din(name, shape, dtype):
        return nc.dram_tensor(name, list(shape), dtype, kind="ExternalInput").ap()

    xw = din("xw", [2048, W], F32)
    xw2 = din("xw2", [6, 128, 16 * 512], F32)
    wk_h = din("wk_t", [HKV, 128, 16, 128], F32)
    wq_h = din("wq_t", [16, 128, 16, 128], F32)
    wo_h = din("wo_t", [16, 128, 16, 128], F32)
    wv_h = din("wv_t", [16, 128, 512], F32)
    gu_h = din("gu_t", [64, 128, 2, 16, 128], U8)
    dn_h = din("down_t", [16, 128, 64, 128], U8)
    gsc_h = din("gsc", [128, 64], F32)
    usc_h = din("usc", [128, 64], F32)
    dsc_h = din("dsc", [128, 16], F32)
    cos_hi = din("cos_hi", [128, W], F32)
    cos_lo = din("cos_lo", [128, W], F32)
    sin_hi = din("sin_hi", [128, W], F32)
    sin_lo = din("sin_lo", [128, W], F32)
    tab_h = {}
    for nm, cols in (("diagT", H * CH), ("qdec", H * CH), ("kdec", H), ("blk", H), ("kd2", H * NCP)):
        tab_h[nm] = (din(nm + "_hi", [128, cols], F32), din(nm + "_lo", [128, cols], F32))
    onesf_h = din("onesf", [128, 128], BF16)

    out_h = nc.dram_tensor("out", [2048, T], F32, kind="ExternalOutput").ap()
    tap_h = {}
    for tname, shape in (("q", [128, H * T]), ("k", [128, HKV * T]), ("vt", [128, NCO * 512]),
                         ("o", [128, H * T]), ("h1", [128, 16 * T]), ("r1", [128, 16 * T]),
                         ("kvin", [128, H * CH]), ("act", [128, 64 * 512]), ("s128", [128, 512])):
        if tname in taps:
            tap_h[tname] = nc.dram_tensor("tap_" + tname, shape, F32, kind="ExternalOutput").ap()

    gsc = nc.alloc_sbuf_tensor("gsc_s", [128, 64], F32).ap()
    usc = nc.alloc_sbuf_tensor("usc_s", [128, 64], F32).ap()
    dsc = nc.alloc_sbuf_tensor("dsc_s", [128, 16], F32).ap()
    onesf = nc.alloc_sbuf_tensor("onesf_s", [128, 128], BF16).ap()
    blkb = nc.alloc_sbuf_tensor("blk_s", [128, H], F32).ap()
    kdecb = nc.alloc_sbuf_tensor("kdec_s", [128, H], F32).ap()
    kd2b = nc.alloc_sbuf_tensor("kd2_s", [128, H * NCP], F32).ap()
    kvS_h = nc.alloc_sbuf_tensor("kvS", [128, H * 128], F32)
    kvS = kvS_h.ap()
    kvSb_h = nc.alloc_sbuf_tensor("kvSb", [128, H * 128], BF16)
    kvSb = kvSb_h.ap()

    base = (nc.sbuf_base + 31) & ~31
    off = [base]

    def at(name, shape, dtype, offset):
        return nc.alloc_sbuf_tensor_at(name, list(shape), dtype, offset=offset).ap()

    def region(sz):
        o = off[0]
        off[0] += sz
        return o

    R0 = region(16384)
    R1 = region(8192)
    R2 = region(24576)
    R3 = region(32768)
    R4 = region(65536)
    R5 = region(49152)
    assert off[0] <= 229344 - 64, f"sbuf overflow: {off[0]}"

    wvT = at("wvT", [128, 16 * 512], BF16, R0)
    h1fC = at("h1fC", [128, 2 * T], F32, R0)
    gub = at("gub", [128, 2 * 2 * 2048], BF16, R0 + 8192)
    wst = at("wst", [128, 2 * 16 * 128], BF16, R1)

    diagTb = at("diagTb", [128, H * CH], F32, R2)
    qdecb = at("qdecb", [128, H * CH], F32, R2 + 8192)
    cosO = at("cosO", [128, T], F32, R2 + 16384)
    sinO = at("sinO", [128, T], F32, R2 + 20480)
    h1fB = at("h1fB", [128, 6 * T], F32, R2)

    xb = at("xb", [128, 16 * T], BF16, R3)
    h1b = at("h1b", [128, 16 * T], BF16, R3)

    xpb = at("xpb", [128, 2 * 16 * 512], BF16, R4)
    ropeC = at("ropeC", [128, 2 * 512], F32, R4 + 32768)
    ropeS = at("ropeS", [128, 2 * 512], F32, R4 + 36864)
    ropeCl = at("ropeCl", [128, 2 * 512], F32, R4 + 40960)
    ropeSl = at("ropeSl", [128, 2 * 512], F32, R4 + 45056)
    ktp = at("ktp", [128, 4 * 512], BF16, R4 + 49152)
    vtp = at("vtp", [128, 4 * 512], BF16, R4 + 53248)
    rsc1 = at("rsc1", [128, 512], F32, R4 + 57344)
    rsc2 = at("rsc2", [128, 512], F32, R4 + 59392)
    kdtmp = at("kdtmp", [128, 2 * 128], BF16, R4 + 59392)
    lob = rsc1
    kroP = at("kroP", [128, 4 * 512], BF16, R4 + 61440)
    lob2 = at("lob2", [128, 512], F32, R4 + 53248)
    kro = at("kro", [128, HKV * T], BF16, R4)
    ktm = at("ktm", [128, NCO * 512], BF16, R4 + 8192)
    vtm = at("vtm", [128, NCO * 512], BF16, R4 + 16384)
    ofm = at("ofm", [128, H * T], BF16, R4 + 24576)
    xf = at("xf", [128, 2 * T], F32, R4 + 57344)
    rbt = at("rbt", [128, 2 * T], BF16, R4 + 57344)
    sqt = at("sqt", [128, 2 * T], BF16, R4 + 61440)
    act_s = at("act_s", [128, 64 * 512], BF16, R4)

    qro = at("qro", [128, H * T], BF16, R5)
    h1fA = at("h1fA", [128, 8 * T], F32, R5)
    wkT = at("wkT", [128, HKV * 16 * 128], BF16, R5 + 32768)
    s2a = at("s2a", [128, 8 * 128], BF16, R5 + 32768)
    qda = at("qda", [128, 8 * 128], BF16, R5 + 32768 + 2048)
    kdt2 = at("kdt2", [128, 8 * 128], BF16, R5 + 32768 + 4096)
    gsub = at("gsub", [128, 512], F32, R5 + 32768 + 6144)
    gact = at("gact", [128, 512], F32, R5 + 32768 + 8192)
    usub = at("usub", [128, 512], F32, R5 + 32768 + 10240)
    rbt2 = at("rbt2", [128, 512], BF16, R5 + 32768 + 12288)
    sqt2 = at("sqt2", [128, 512], BF16, R5 + 32768 + 13312)
    kdall = at("kdall", [128, 16 * 128], BF16, R1)
    kroP2 = at("kroP2", [128, 4 * 512], BF16, R1 + 4096)
    kvs_addr = nc.lookup_mloc(kvS_h).addr
    kvsb_addr = nc.lookup_mloc(kvSb_h).addr
    s128d = kvS[:, 0:512]
    s128b = at("s128b", [128, 512], F32, kvs_addr + 2048)
    dbr = at("dbr", [128, 16 * 128], BF16, kvs_addr + 4096)
    dbr2 = at("dbr2", [128, 16 * 128], BF16, kvsb_addr)

    def dbr_slot(g):
        return gub[:, (g % 4) * 2048:((g % 4) + 1) * 2048]

    def h1f(dp):
        if dp < 8:
            return h1fA[:, dp * T:(dp + 1) * T]
        if dp < 14:
            return h1fB[:, (dp - 8) * T:(dp - 7) * T]
        return h1fC[:, (dp - 14) * T:(dp - 13) * T]

    def act_ap(ic):
        return act_s[:, ic * 512:(ic + 1) * 512]

    pbig = [nc.alloc_psum_tensor(f"p{i}", [128, 512], F32).ap() for i in range(4)]
    psml = [nc.alloc_psum_tensor(f"q{i}", [128, 128], F32).ap() for i in range(4)]
    pA, pB, pC, pD = pbig

    em = Em(nc)
    for s in ("a", "a0", "a1", "gw", "gx0", "gx1", "gq0", "gq1", "gu0", "gu1", "gu2", "gu3",
              "gd0", "gd1", "gd2", "gd3", "st", "sx0", "sx1", "so", "pe", "dv", "ac", "po"):
        em.sem(s)

    bfree = {}

    def bank_wait(key):
        em.waitev("tensor", bfree.get(key))

    def bank_evac(key, sem):
        bfree[key] = (sem, em.c(sem))

    def dma(eng, sem, out, in_):
        em.emit(eng, lambda e, o=out, i=in_: e.dma_start(out=o, in_=i), inc=sem, amt=16)
        return (sem, em.c(sem))

    _tq = [0]

    def dmat(out, in_):
        if SKIP_TRANS:
            return ("st", em.c("st"))
        qe = "sync"
        em.emit(qe, lambda e, o=out, i=in_: e.dma_start_transpose(o, i), inc="st", amt=16)
        return ("st", em.c("st"))

    def mm(out, lhsT, rhs, start, stop, mark=False):
        v = em.emit("tensor",
                    lambda e, o=out, l=lhsT, r=rhs, st=start, sp=stop: e.matmul(o, l, r, start=st, stop=sp),
                    inc=("pe" if mark else None))
        return ("pe", v) if mark else None

    _ENG_SEM = {"vector": "dv", "scalar": "ac", "gpsimd": "po"}

    def tt(eng, out, in0, in1, op, inc=True):
        sem = _ENG_SEM[eng] if inc else None
        v = em.emit(eng, lambda e, o=out, a=in0, b=in1, p=op: e.tensor_tensor(o, a, b, p), inc=sem)
        return (sem, v) if inc else None

    def ts(eng, out, in0, scalar, op, inc=True):
        sem = _ENG_SEM[eng] if inc else None
        v = em.emit(eng, lambda e, o=out, a=in0, s=scalar, p=op: e.tensor_single_scalar(o, a, s, p), inc=sem)
        return (sem, v) if inc else None

    def stt(eng, out, in0, scalar, in1, op0, op1, inc=True):
        sem = _ENG_SEM[eng] if inc else None
        v = em.emit(eng, lambda e, o=out, a=in0, s=scalar, b=in1, p0=op0, p1=op1:
                    e.scalar_tensor_tensor(o, a, s, b, p0, p1), inc=sem)
        return (sem, v) if inc else None

    def cp(eng, out, in_, inc=True):
        sem = _ENG_SEM[eng] if inc else None
        if eng == "scalar":
            v = em.emit(eng, lambda e, o=out, i=in_: e.copy(o, i), inc=sem)
        else:
            v = em.emit(eng, lambda e, o=out, i=in_: e.tensor_copy(o, i), inc=sem)
        return (sem, v) if inc else None

    def act_fn(out, in_, func, scale=None):
        if scale is None:
            v = em.emit("scalar", lambda e, o=out, i=in_, f=func: e.activation(o, i, f), inc="ac")
        else:
            v = em.emit("scalar", lambda e, o=out, i=in_, f=func, s=scale: e.activation(o, i, f, scale=s), inc="ac")
        return ("ac", v)

    def rope_fm(psrc, cosA, sinA, out_bf):
        tt("vector", rsc1[:], psrc, cosA, ALU.mult, inc=False)
        tt("vector", rsc2[0:64, :], psrc[64:128, :], sinA[0:64, :], ALU.mult, inc=False)
        tt("vector", rsc2[64:128, :], psrc[0:64, :], sinA[64:128, :], ALU.mult, inc=False)
        tt("vector", out_bf[0:64, :], rsc1[0:64, :], rsc2[0:64, :], ALU.subtract, inc=False)
        return tt("vector", out_bf[64:128, :], rsc1[64:128, :], rsc2[64:128, :], ALU.add)

    for dst, src in ((gsc, gsc_h), (usc, usc_h), (dsc, dsc_h), (onesf, onesf_h)):
        dma("scalar", "a", dst, src)

    def combine(dst_full, hi_h, lo_h, cols):
        em.wait("scalar", "dv")
        dma("scalar", "a", dst_full, hi_h)
        for j0 in range(0, cols, 512):
            jw = min(512, cols - j0)
            em.wait("scalar", "dv")
            dma("scalar", "a", lob[:, 0:jw], lo_h[:, j0:j0 + jw])
            em.wait("vector", "a")
            tt("vector", dst_full[:, j0:j0 + jw], dst_full[:, j0:j0 + jw], lob[:, 0:jw], ALU.add)

    combine(kd2b, *tab_h["kd2"], H * NCP)
    combine(cosO, cos_hi[:, PRED:W], cos_lo[:, PRED:W], T)
    combine(sinO, sin_hi[:, PRED:W], sin_lo[:, PRED:W], T)
    combine(diagTb, *tab_h["diagT"], H * CH)
    combine(qdecb, *tab_h["qdec"], H * CH)
    combine(kdecb, *tab_h["kdec"], H)
    combine(blkb, *tab_h["blk"], H)

    for e in range(HKV):
        dma("gpsimd", "gw", wkT[:, e * 2048:(e + 1) * 2048], wk_h[e].rearrange("p d f -> p (d f)"))
    dma("gpsimd", "gw", wvT[:].rearrange("p (d f) -> p d f", d=16), wv_h.rearrange("d p f -> p d f"))
    dma("gpsimd", "gw", xb[:].rearrange("p (d t) -> p d t", d=16),
        xw[:, PRED:W].rearrange("(d p) t -> p d t", p=128))
    g_setup = ("gw", em.c("gw"))
    a_setup = ("a", em.c("a"))

    ev_kv0 = ("dv", em.emit("vector", lambda e: e.memset(kvS[:], 0.0), inc="dv"))

    pe_chunk_done = {}
    chunk_po_done = None
    chunk_pe_done = None
    ktp_prev1 = None
    ktp_prev2 = None
    u_cons = {}
    for pc in range(6):
        buf = pc % 2
        if pc >= 2:
            em.waitev("gpsimd", pe_chunk_done[pc - 2])
            em.waitev("scalar", pe_chunk_done[pc - 2])
        ev_x = dma("gpsimd", f"gx{buf}", xpb[:, buf * 8192:(buf + 1) * 8192], xw2[pc])
        dma("scalar", f"a{buf}", ropeC[:, buf * 512:(buf + 1) * 512], cos_hi[:, pc * 512:(pc + 1) * 512])
        dma("scalar", f"a{buf}", ropeCl[:, buf * 512:(buf + 1) * 512], cos_lo[:, pc * 512:(pc + 1) * 512])
        dma("scalar", f"a{buf}", ropeS[:, buf * 512:(buf + 1) * 512], sin_hi[:, pc * 512:(pc + 1) * 512])
        ev_sl = dma("scalar", f"a{buf}", ropeSl[:, buf * 512:(buf + 1) * 512], sin_lo[:, pc * 512:(pc + 1) * 512])
        em.waitev("vector", ev_sl)
        cosP = ropeC[:, buf * 512:(buf + 1) * 512]
        sinP = ropeS[:, buf * 512:(buf + 1) * 512]
        tt("vector", cosP, cosP, ropeCl[:, buf * 512:(buf + 1) * 512], ALU.add, inc=False)
        ev_rope = tt("vector", sinP, sinP, ropeSl[:, buf * 512:(buf + 1) * 512], ALU.add)

        xp = xpb[:, buf * 8192:(buf + 1) * 8192]
        if pc == 0:
            em.waitev("tensor", g_setup)
        em.waitev("tensor", ev_x)
        kroB = kroP if buf == 0 else kroP2
        ev_krop = {}
        for e in range(HKV):
            bank_wait(f"p{e}")
            for d in range(16):
                ev_mm = mm(pbig[e], wkT[:, (e * 16 + d) * 128:(e * 16 + d + 1) * 128],
                           xp[:, d * 512:(d + 1) * 512], d == 0, d == 15, mark=(d == 15))
            em.waitev("vector", ev_mm)
            em.waitev("vector", ev_rope)
            em.waitev("vector", ktp_prev2)
            ev_krop[e] = rope_fm(pbig[e], cosP, sinP, kroB[:, e * 512:(e + 1) * 512])
            bank_evac(f"p{e}", "dv")
        em.waitev("sync", chunk_pe_done)
        em.waitev("scalar", chunk_pe_done)
        ev_ktp = {}
        for tsp in range(4):
            for e in range(HKV):
                em.waitev("sync", ev_krop[e])
                em.waitev("scalar", ev_krop[e])
                ev_ktp[tsp] = dmat(ktp[:, (tsp * 4 + e) * 128:(tsp * 4 + e + 1) * 128],
                                   kroB[:, e * 512 + tsp * 128: e * 512 + (tsp + 1) * 128])
        ktp_prev2 = ktp_prev1
        ktp_prev1 = ev_ktp[3]
        ev_vtp = {}
        for tts in range(4):
            bank_wait(f"p{tts}")
            for d in range(16):
                ev_mm = mm(pbig[tts], xp[:, d * 512 + tts * 128: d * 512 + (tts + 1) * 128],
                           wvT[:, d * 512:(d + 1) * 512], d == 0, d == 15, mark=(d == 15))
            em.waitev("scalar", ev_mm)
            em.waitev("scalar", chunk_pe_done)
            ev_vtp[tts] = cp("scalar", vtp[:, tts * 512:(tts + 1) * 512], pbig[tts])
            bank_evac(f"p{tts}", "ac")
        ev_kb = {}
        u_prev = u_cons
        u_cons = {}
        for m in range(64 + 16):
            mc = m - 16
            if 0 <= mc < 64:
                hh, sc = mc // 4, mc % 4
                pu = psml[hh % 4]
                em.waitev("tensor", ev_kb[mc])
                em.waitev("tensor", ev_vtp[sc])
                if sc == 0:
                    bank_wait(f"q{hh % 4}")
                u_cons[mc] = mm(pu, kdall[:, (mc % 16) * 128:(mc % 16 + 1) * 128],
                                vtp[:, sc * 512 + (hh // 4) * 128: sc * 512 + (hh // 4) * 128 + 128],
                                sc == 0, sc == 3, mark=True)
                if sc == 3:
                    em.waitev("vector", u_cons[mc])
                    em.waitev("vector", ev_kv0)
                    tt("vector", kvS[:, hh * 128:(hh + 1) * 128],
                       kvS[:, hh * 128:(hh + 1) * 128], pu, ALU.add)
                    bank_evac(f"q{hh % 4}", "dv")
            if m < 64:
                hh, sc = m // 4, m % 4
                em.waitev("vector", ev_ktp[3])
                em.waitev("vector", u_cons.get(m - 16) if m >= 16 else u_prev.get(m + 48))
                ev_kb[m] = ts("vector", kdall[:, (m % 16) * 128:(m % 16 + 1) * 128],
                              ktp[:, (sc * 4 + (hh // 4)) * 128:(sc * 4 + (hh // 4) + 1) * 128],
                              kd2b[:, hh * NCP + (pc * 4 + sc): hh * NCP + (pc * 4 + sc) + 1], ALU.mult)
        pe_chunk_done[pc] = ("pe", em.c("pe"))
        chunk_pe_done = pe_chunk_done[pc]
        chunk_po_done = ("po", em.c("po"))
    ev_kvin = ("dv", em.c("dv"))

    if "kvin" in tap_h:
        em.waitev("sync", ev_kvin)
        dma("sync", "so", tap_h["kvin"], kvS[:])

    def finish():
        em.emit("vector", lambda e: e.memset(h1fA[:, 0:T], 0.0), inc="dv")
        em.waitev("sync", ("dv", em.c("dv")))
        em.waitev("sync", ("ac", em.c("ac")))
        em.waitev("sync", ("po", em.c("po")))
        dma("sync", "so", out_h[0:128, :], h1fA[:, 0:T])
        em.wait("sync", "so", em.c("so"))
        with nc.Block() as block:
            for eng in ENGS:
                ops = em.q[eng]

                def run(e, ops=ops):
                    for op in ops:
                        op(e)

                getattr(block, eng)(run)
        return nc, None

    if STOP == 1:
        return finish()

    em.waitev("tensor", g_setup)
    ev_krop_own = {}
    for e in range(HKV):
        for tq in range(2):
            g_i = e * 2 + tq
            bank_wait(f"p{g_i % 4}")
            for d in range(16):
                ev_mm = mm(pbig[g_i % 4], wkT[:, (e * 16 + d) * 128:(e * 16 + d + 1) * 128],
                           xb[:, d * T + tq * 512: d * T + (tq + 1) * 512], d == 0, d == 15, mark=(d == 15))
            em.waitev("vector", ev_mm)
            ev_krop_own[(e, tq)] = rope_fm(pbig[g_i % 4], cosO[:, tq * 512:(tq + 1) * 512],
                                           sinO[:, tq * 512:(tq + 1) * 512],
                                           kro[:, e * T + tq * 512: e * T + (tq + 1) * 512])
            bank_evac(f"p{g_i % 4}", "dv")
    em.waitev("sync", chunk_pe_done)
    em.waitev("scalar", chunk_pe_done)
    ev_ktm = None
    for c in range(NCO):
        for e in range(HKV):
            em.waitev("sync", ev_krop_own[(e, c // 4)])
            em.waitev("scalar", ev_krop_own[(e, c // 4)])
            ev_ktm = dmat(ktm[:, c * 512 + e * 128: c * 512 + (e + 1) * 128],
                          kro[:, e * T + c * 128: e * T + (c + 1) * 128])
    ev_vtm_all = None
    for tts in range(NCO):
        bank_wait(f"p{tts % 4}")
        for d in range(16):
            ev_mm = mm(pbig[tts % 4], xb[:, d * T + tts * 128: d * T + (tts + 1) * 128],
                       wvT[:, d * 512:(d + 1) * 512], d == 0, d == 15, mark=(d == 15))
        em.waitev("scalar", ev_mm)
        em.waitev("scalar", chunk_pe_done)
        ev_vtm_all = cp("scalar", vtm[:, tts * 512:(tts + 1) * 512], pbig[tts % 4])
        bank_evac(f"p{tts % 4}", "ac")
    ev_q_done = {}
    for e in range(16):
        wb = wst[:, (e % 2) * 2048:(e % 2 + 1) * 2048]
        if e >= 2:
            em.waitev("gpsimd", ev_q_done[e - 2])
        else:
            em.waitev("gpsimd", pe_chunk_done[5])
        ev_w = dma("gpsimd", f"gq{e % 2}", wb, wq_h[e].rearrange("p d f -> p (d f)"))
        em.waitev("tensor", ev_w)
        for tq in range(2):
            g_i = e * 2 + tq
            bank_wait(f"p{g_i % 4}")
            for d in range(16):
                ev_mm = mm(pbig[g_i % 4], wb[:, d * 128:(d + 1) * 128],
                           xb[:, d * T + tq * 512: d * T + (tq + 1) * 512], d == 0, d == 15, mark=(d == 15))
            ev_q_done[e] = ev_mm
            em.waitev("vector", ev_mm)
            rope_fm(pbig[g_i % 4], cosO[:, tq * 512:(tq + 1) * 512], sinO[:, tq * 512:(tq + 1) * 512],
                    qro[:, e * T + tq * 512: e * T + (tq + 1) * 512])
            bank_evac(f"p{g_i % 4}", "dv")
    ev_own = ("dv", em.c("dv"))

    if "q" in tap_h:
        em.waitev("gpsimd", ev_own)
        dma("gpsimd", "so", tap_h["q"], qro[:])
    if "k" in tap_h:
        em.waitev("gpsimd", ev_own)
        dma("gpsimd", "so", tap_h["k"], kro[:])
    if "vt" in tap_h:
        em.waitev("gpsimd", ev_vtm_all)
        dma("gpsimd", "so", tap_h["vt"], vtm[:])

    if STOP == 2:
        return finish()

    em.waitev("vector", ev_kvin)
    ev_kvsb = cp("vector", kvSb[:], kvS[:])

    its = [(c, hh) for c in range(NCO) for hh in range(H)]
    ev_s2, ev_qd, ev_kb2, ev_omm, ev_umm, ev_stt, ev_cast = {}, {}, {}, {}, {}, {}, {}

    def scan_front(i):
        c, hh = its[i]
        e = hh // 4
        tog = i % 8
        if i >= 8:
            em.waitev("vector", ev_omm.get(i - 8))
            em.waitev("scalar", ev_umm.get(i - 8))
        em.waitev("scalar", ev_ktm)
        if i == 0:
            em.waitev("scalar", ev_kv0)
        ev_qd[i] = tt("vector", qda[:, tog * 128:(tog + 1) * 128],
                      qro[:, hh * T + c * 128:hh * T + (c + 1) * 128],
                      qdecb[:, hh * 128:(hh + 1) * 128], ALU.mult)
        ev_kb2[i] = ("ac", em.emit("scalar", lambda e_, o=kdt2[:, tog * 128:(tog + 1) * 128],
                                   a=ktm[:, c * 512 + e * 128: c * 512 + (e + 1) * 128],
                                   m=kdecb[:, hh:hh + 1]: e_.mul(o, a, m), inc="ac"))
        pS = psml[i % 2]
        bank_wait(f"q{i % 2}")
        if i == 0:
            em.waitev("tensor", ev_ktm)
        ev = mm(pS, kro[:, e * T + c * 128:e * T + (c + 1) * 128],
                qro[:, hh * T + c * 128:hh * T + (c + 1) * 128], True, True, mark=True)
        em.waitev("vector", ev)
        ev_s2[i] = tt("vector", s2a[:, tog * 128:(tog + 1) * 128], pS,
                      diagTb[:, hh * 128:(hh + 1) * 128], ALU.mult)
        bfree[f"q{i % 2}"] = ev_s2[i]

    def scan_back(i):
        c, hh = its[i]
        e = hh // 4
        tog = i % 8
        pO = psml[2 + i % 2]
        pS = psml[i % 2]
        bank_wait(f"q{2 + i % 2}")
        em.waitev("tensor", ev_s2[i])
        em.waitev("tensor", ev_qd[i])
        em.waitev("tensor", ev_cast.get(i - H, ev_kvsb))
        mm(pO, vtm[:, c * 512 + e * 128: c * 512 + (e + 1) * 128],
           s2a[:, tog * 128:(tog + 1) * 128], True, False)
        ev_omm[i] = mm(pO, kvSb[:, hh * 128:(hh + 1) * 128],
                       qda[:, tog * 128:(tog + 1) * 128], False, True, mark=True)
        em.waitev("tensor", ev_kb2[i])
        ev_umm[i] = mm(pS, kdt2[:, tog * 128:(tog + 1) * 128],
                       vtm[:, c * 512 + e * 128: c * 512 + (e + 1) * 128], True, True, mark=True)
        em.waitev("scalar", ev_omm[i])
        cp("scalar", ofm[:, hh * T + c * 128: hh * T + (c + 1) * 128], pO)
        bank_evac(f"q{2 + i % 2}", "ac")
        em.waitev("vector", ev_umm[i])
        ev_stt[i] = stt("vector", kvS[:, hh * 128:(hh + 1) * 128], kvS[:, hh * 128:(hh + 1) * 128],
                        blkb[:, hh:hh + 1], pS, ALU.mult, ALU.add)
        bfree[f"q{i % 2}"] = ev_stt[i]
        em.waitev("scalar", ev_stt[i])
        ev_cast[i] = cp("scalar", kvSb[:, hh * 128:(hh + 1) * 128], kvS[:, hh * 128:(hh + 1) * 128])

    for i in range(len(its)):
        scan_front(i)
        if i > 0:
            scan_back(i - 1)
    scan_back(len(its) - 1)
    ev_scan = ("ac", em.c("ac"))

    if "o" in tap_h:
        em.waitev("gpsimd", ev_scan)
        dma("gpsimd", "so", tap_h["o"], ofm[:])

    if STOP == 3:
        return finish()

    ev_op_done = {}
    ev_xf_used = {}
    for dp in range(16):
        wb = wst[:, (dp % 2) * 2048:(dp % 2 + 1) * 2048]
        if dp < 2:
            em.waitev("gpsimd", ("pe", em.c("pe")))
            em.waitev("sync", ev_scan)
        else:
            em.waitev("gpsimd", ev_op_done[dp - 2])
            em.waitev("sync", ev_xf_used[dp - 2])
        ev_w = dma("gpsimd", f"gq{dp % 2}", wb, wo_h[dp].rearrange("p d f -> p (d f)"))
        ev_xf = dma("sync", f"sx{dp % 2}", xf[:, (dp % 2) * T:(dp % 2 + 1) * T],
                    xw[:, PRED:W].rearrange("(d p) t -> d p t", p=128)[dp])
        em.waitev("tensor", ev_w)
        if dp == 0:
            em.waitev("tensor", ev_scan)
        for tq in range(2):
            bk = (dp * 2 + tq) % 4
            bank_wait(f"p{bk}")
            for e in range(16):
                ev_mm = mm(pbig[bk], wb[:, e * 128:(e + 1) * 128],
                           ofm[:, e * T + tq * 512: e * T + (tq + 1) * 512], e == 0, e == 15, mark=(e == 15))
            ev_op_done[dp] = ev_mm
            em.waitev("vector", ev_mm)
            em.waitev("vector", ev_xf)
            ev = tt("vector", h1f(dp)[:, tq * 512:(tq + 1) * 512], pbig[bk],
                    xf[:, (dp % 2) * T + tq * 512:(dp % 2) * T + (tq + 1) * 512], ALU.add)
            bank_evac(f"p{bk}", "dv")
        ev_xf_used[dp] = ev

    if "r1" in tap_h:
        em.waitev("sync", ("dv", em.c("dv")))
        for dp in range(16):
            dma("sync", "so", tap_h["r1"][:, dp * T:(dp + 1) * T], h1f(dp))
        em.wait("vector", "so")

    def layer_norm(get_tile, tq_count, rcast, scast, nbuf, ready_dv, ready_po=None, post_tq=None):
        stat_m = (pC, pD)
        stat_s = (pA, pB)
        em.waitev("scalar", ready_dv)
        em.waitev("vector", ready_dv)
        if ready_po is not None:
            em.waitev("scalar", ready_po)
            em.waitev("vector", ready_po)
        ev_stats = {}
        for dp in range(16):
            w = tq_count * 512
            rb = rcast[:, (dp % nbuf) * w:((dp % nbuf) + 1) * w]
            sq = scast[:, (dp % nbuf) * w:((dp % nbuf) + 1) * w]
            if dp >= nbuf:
                em.waitev("scalar", ev_stats[dp - nbuf])
                em.waitev("vector", ev_stats[dp - nbuf])
            ev_rb = cp("scalar", rb, get_tile(dp))
            ev_sq = tt("vector", sq, get_tile(dp), get_tile(dp), ALU.mult)
            em.waitev("tensor", ev_rb)
            em.waitev("tensor", ev_sq)
            if dp == 0:
                for tq in range(tq_count):
                    bank_wait(f"p{2 + tq}")
                    bank_wait(f"p{0 + tq}")
            for tq in range(tq_count):
                mm(stat_m[tq], onesf, rb[:, tq * 512:(tq + 1) * 512], dp == 0, dp == 15)
                ev_stats[dp] = mm(stat_s[tq], onesf, sq[:, tq * 512:(tq + 1) * 512],
                                  dp == 0, dp == 15, mark=(tq == tq_count - 1))
        ev_ap_d = None
        ev_ap_p = None
        for tq in range(tq_count):
            em.waitev("vector", ev_stats[15])
            ts("vector", gsub[:], stat_m[tq], 1.0 / 2048.0, ALU.mult, inc=False)
            ts("vector", gact[:], stat_s[tq], 1.0 / 2048.0, ALU.mult, inc=False)
            tt("vector", usub[:], gsub[:], gsub[:], ALU.mult, inc=False)
            tt("vector", gact[:], gact[:], usub[:], ALU.subtract, inc=False)
            ev = ts("vector", gact[:], gact[:], EPS, ALU.add)
            bank_evac(f"p{2 + tq}", "dv")
            bank_evac(f"p{0 + tq}", "dv")
            em.waitev("scalar", ev)
            act_fn(rsc2[:], gact[:], AF.Ln)
            ev = act_fn(usub[:], rsc2[:], AF.Exp, scale=-0.5)
            em.waitev("vector", ev)
            tt("vector", rsc2[:], usub[:], usub[:], ALU.mult, inc=False)
            tt("vector", rsc2[:], rsc2[:], gact[:], ALU.mult, inc=False)
            em.emit("vector", lambda e, o=rsc2: e.tensor_scalar(
                o[:], o[:], -0.5, 1.5, ALU.mult, ALU.add))
            tt("vector", gact[:], usub[:], rsc2[:], ALU.mult, inc=False)
            for dp in range(16):
                tile = get_tile(dp)[:, tq * 512:(tq + 1) * 512]
                tt("vector", rsc1[:], tile, gsub[:], ALU.subtract, inc=False)
                ev_ap_d = tt("vector", tile, rsc1[:], gact[:], ALU.mult)
            if post_tq is not None:
                post_tq(tq)
        return ev_ap_d, ev_ap_d

    em.waitev("tensor", a_setup)
    em.waitev("scalar", a_setup)
    em.waitev("vector", a_setup)
    ev_r1 = ("dv", em.c("dv"))
    ev_h1b_half = {}

    def _ln1_post(tq):
        for dp in range(16):
            cp("vector", h1b[:, dp * T + tq * 512: dp * T + (tq + 1) * 512],
               h1f(dp)[:, tq * 512:(tq + 1) * 512], inc=(dp == 15))
        ev_h1b_half[tq] = ("dv", em.c("dv"))

    layer_norm(lambda dp: h1f(dp), 2, rbt, sqt, 2, ev_r1, post_tq=_ln1_post)
    ev_h1b = ev_h1b_half[1]
    ev_h1b_d = ev_h1b_half[1]

    if "h1" in tap_h:
        em.waitev("sync", ev_h1b)
        em.waitev("sync", ev_h1b_d)
        for dp in range(16):
            dma("sync", "so", tap_h["h1"][:, dp * T:(dp + 1) * T], h1f(dp))

    if STOP == 4:
        return finish()

    ev_gu = {}
    ev_gg = {}
    ev_act = {}
    for th in range(2):
        em.waitev("tensor", ev_h1b_half[th])
        bank_wait("p2")
        for d in range(16):
            ev_mm = mm(pC, onesf, h1b[:, d * T + th * 512: d * T + (th + 1) * 512],
                       d == 0, d == 15, mark=(d == 15))
        em.waitev("vector", ev_mm)
        ev_s128 = ts("vector", s128b[:], pC, 128.0, ALU.mult)
        bank_evac("p2", "dv")

        for i in range(64):
            gb = gub[:, (i % 2) * 2048:(i % 2 + 1) * 2048]
            ub = gub[:, 4096 + (i % 2) * 2048: 4096 + (i % 2 + 1) * 2048]
            if (th, i) == (0, 0):
                em.waitev("gpsimd", ev_op_done[15])
            else:
                prevg = ev_gg.get((th, i - 2)) if i >= 2 else ev_gg.get((th - 1, 62 + i))
                if prevg is not None:
                    em.waitev("gpsimd", prevg)
                if th == 1 and i == 0:
                    em.waitev("gpsimd", prev_th_grp)
            ev_wg = dma("gpsimd", f"gu{i % 2}", gb, gu_h[i].rearrange("p g h f -> g p (h f)")[0])
            if (th, i) != (0, 0):
                prevu = ev_gu.get((th, i - 2)) if i >= 2 else ev_gu.get((th - 1, 62 + i))
                if prevu is not None:
                    em.waitev("gpsimd", prevu)
            ev_wu = dma("gpsimd", f"gu{2 + i % 2}", ub, gu_h[i].rearrange("p g h f -> g p (h f)")[1])
            em.waitev("tensor", ev_wg)
            if i == 0:
                em.waitev("tensor", ev_s128)
            bg, bu = (0, 1) if i % 2 == 0 else (2, 3)
            bank_wait(f"p{bg}")
            for hc in range(16):
                ev_g = mm(pbig[bg], gb[:, hc * 128:(hc + 1) * 128],
                          h1b[:, hc * T + th * 512: hc * T + (th + 1) * 512], hc == 0, hc == 15,
                          mark=(hc == 15))
            ev_gg[(th, i)] = ev_g
            em.waitev("tensor", ev_wu)
            bank_wait(f"p{bu}")
            for hc in range(16):
                ev_u = mm(pbig[bu], ub[:, hc * 128:(hc + 1) * 128],
                          h1b[:, hc * T + th * 512: hc * T + (th + 1) * 512], hc == 0, hc == 15,
                          mark=(hc == 15))
            ev_gu[(th, i)] = ev_u
            em.waitev("vector", ev_g)
            em.waitev("vector", ev_s128)
            ev = tt("vector", gsub[:], pbig[bg], s128b[:], ALU.subtract)
            bank_evac(f"p{bg}", "dv")
            em.waitev("scalar", ev)
            ev_si = act_fn(gact[:], gsub[:], AF.Silu, scale=gsc[:, i:i + 1])
            em.waitev("vector", ev_u)
            ev = tt("vector", usub[:], pbig[bu], s128b[:], ALU.subtract)
            bank_evac(f"p{bu}", "dv")
            em.waitev("vector", ev_si)
            ev_act[(th, i)] = stt("vector", act_ap(i), usub[:], usc[:, i:i + 1], gact[:],
                                  ALU.mult, ALU.mult)

        bank_wait("p0")
        em.waitev("tensor", ev_act[(th, 63)])
        for ic in range(64):
            ev_mm = mm(pA, onesf, act_ap(ic), ic == 0, ic == 63, mark=(ic == 63))
        em.waitev("vector", ev_mm)
        ev_s128d = ts("vector", s128d, pA, 128.0, ALU.mult)
        bank_evac("p0", "dv")

        ev_dn = {}
        ev_grp = {}

        def dn_dma(g):
            hp2, grp2 = g // 4, g % 4
            if g >= 4:
                em.waitev("gpsimd", ev_grp.get(g - 4))
            elif g == 0:
                em.waitev("gpsimd", ev_gg[(th, 63)])
                em.waitev("gpsimd", ev_gu[(th, 63)])
            ev_dn[g] = dma("gpsimd", f"gd{g % 4}",
                           dbr_slot(g),
                           dn_h[hp2].rearrange("p c f -> p (c f)")[:, grp2 * 2048:(grp2 + 1) * 2048])

        dn_dma(0)
        dn_dma(1)
        dn_dma(2)
        dn_dma(3)
        for hp in range(16):
            bk = 2 + hp % 2
            bank_wait(f"p{bk}")
            for grp in range(4):
                g = hp * 4 + grp
                em.waitev("tensor", ev_dn[g])
                if hp == 0 and grp == 0:
                    em.waitev("tensor", ev_s128d)
                for j in range(16):
                    ic = grp * 16 + j
                    ev_grp[g] = mm(pbig[bk], dbr_slot(g)[:, j * 128:(j + 1) * 128],
                                   act_ap(ic), ic == 0, ic == 63, mark=(j == 15))
                if g + 4 < 64:
                    dn_dma(g + 4)
            em.waitev("vector", ev_grp[hp * 4 + 3])
            em.waitev("vector", ev_s128d)
            tt("vector", gsub[:], pbig[bk], s128d, ALU.subtract, inc=False)
            stt("vector", h1f(hp)[:, th * 512:(th + 1) * 512], gsub[:], dsc[:, hp:hp + 1],
                h1f(hp)[:, th * 512:(th + 1) * 512], ALU.mult, ALU.add)
            bank_evac(f"p{bk}", "dv")
        prev_th_grp = ev_grp[63]

        if th == 0 and "act" in tap_h:
            em.waitev("gpsimd", ev_act[(0, 63)])
            dma("gpsimd", "so", tap_h["act"], act_s[:])
        if th == 0 and "s128" in tap_h:
            em.waitev("sync", ev_s128)
            dma("sync", "so", tap_h["s128"], s128b[:])

        def get2(dp, th=th):
            return h1f(dp)[:, th * 512:(th + 1) * 512]

        ev_r2 = ("dv", em.c("dv"))
        layer_norm(get2, 1, rbt2, sqt2, 1, ev_r2)
        em.waitev("sync", ("dv", em.c("dv")))
        for dp in range(16):
            dma("sync", "so",
                out_h.rearrange("(d p) t -> d p t", p=128)[dp][:, th * 512:(th + 1) * 512],
                h1f(dp)[:, th * 512:(th + 1) * 512])

    em.wait("sync", "so", em.c("so"))

    with nc.Block() as block:
        for eng in ENGS:
            ops = em.q[eng]

            def run(e, ops=ops):
                for op in ops:
                    op(e)

            getattr(block, eng)(run)

    return nc, None



def _prep_shared(wq, wk, wv, wo, gw, uw, dw, gs, us, ds):
    def f32(a):
        return np.ascontiguousarray(np.asarray(a, dtype=np.float32))

    wq, wk, wv, wo = f32(wq), f32(wk), f32(wv), f32(wo)

    def u8(a):
        a = np.ascontiguousarray(np.asarray(a))
        assert a.dtype == np.int32
        return np.ascontiguousarray(a.view(np.uint8)[:, 0::4])

    g8, u8w, d8 = u8(gw), u8(uw), u8(dw)

    d = {}
    d["wq_t"] = np.ascontiguousarray(wq.reshape(16, 128, 16, 128).transpose(0, 3, 2, 1))
    d["wk_t"] = np.ascontiguousarray(wk.reshape(HKV, 128, 16, 128).transpose(0, 3, 2, 1))
    d["wo_t"] = np.ascontiguousarray(wo.reshape(16, 128, 16, 128).transpose(0, 3, 2, 1))
    d["wv_t"] = np.ascontiguousarray(wv.reshape(HKV * 128, 16, 128).transpose(1, 2, 0))
    gt = g8.reshape(64, 128, 16, 128).transpose(0, 3, 2, 1)
    ut = u8w.reshape(64, 128, 16, 128).transpose(0, 3, 2, 1)
    d["gu_t"] = np.ascontiguousarray(np.stack([gt, ut], axis=1).transpose(0, 2, 1, 3, 4))
    d["down_t"] = np.ascontiguousarray(d8.reshape(16, 128, 64, 128).transpose(0, 3, 2, 1))
    d["gsc"] = np.ascontiguousarray(np.asarray(gs, np.float32).reshape(64, 128).T)
    d["usc"] = np.ascontiguousarray(np.asarray(us, np.float32).reshape(64, 128).T)
    d["dsc"] = np.ascontiguousarray(np.asarray(ds, np.float32).reshape(16, 128).T)
    t = _tables()
    for nm in ("diagT", "qdec", "kdec", "blk", "kd2"):
        d[nm + "_hi"], d[nm + "_lo"] = t[nm]
    d["onesf"] = np.ones((128, 128), dtype=ml_dtypes.bfloat16)
    return d


def _prep_core(x, b, r):
    xT = np.asarray(x[b], dtype=np.float32).T
    winw = np.zeros((2048, W), dtype=np.float32)
    start = (r + 1) * T - W
    if start < 0:
        winw[:, -start:] = xT[:, 0:(r + 1) * T]
    else:
        winw[:, :] = xT[:, start:(r + 1) * T]
    (chi, clo), (shi, slo) = _rope_tables(r)
    xw2 = np.ascontiguousarray(
        winw[:, :PRED].reshape(16, 128, 6, 512).transpose(2, 1, 0, 3).reshape(6, 128, 16 * 512))
    return {"xw": np.ascontiguousarray(winw), "xw2": xw2,
            "cos_hi": np.ascontiguousarray(chi), "cos_lo": np.ascontiguousarray(clo),
            "sin_hi": np.ascontiguousarray(shi), "sin_lo": np.ascontiguousarray(slo)}


def kernel(L0_attn_x, L0_attn_wq, L0_attn_wk, L0_attn_wv, L0_attn_wo,
           L0_ffn_gate_w, L0_ffn_gate_scale, L0_ffn_up_w, L0_ffn_up_scale,
           L0_ffn_down_w, L0_ffn_down_scale):
    shared = _prep_shared(L0_attn_wq, L0_attn_wk, L0_attn_wv, L0_attn_wo,
                          L0_ffn_gate_w, L0_ffn_up_w, L0_ffn_down_w,
                          L0_ffn_gate_scale, L0_ffn_up_scale, L0_ffn_down_scale)
    x = np.asarray(L0_attn_x, dtype=np.float32)
    in_maps = []
    for c in range(8):
        m = dict(shared)
        m.update(_prep_core(x, c // 4, c % 4))
        in_maps.append(m)
    nc, _ = build_nc()
    res = run_bass_kernel_spmd(nc, in_maps, core_ids=list(range(8)))
    full = np.empty((B, D, S), dtype=np.float32)
    for c in range(8):
        b, r = c // 4, c % 4
        full[b, :, r * T:(r + 1) * T] = res.results[c]["out"]
    return np.ascontiguousarray(full.transpose(0, 2, 1))


# version 08
# speedup vs baseline: 1.0145x; 1.0145x over previous
import numpy as np
import ml_dtypes
import concourse.bass as bass
import concourse.mybir as mybir
from concourse.bass_utils import run_bass_kernel_spmd

F32 = mybir.dt.float32
BF16 = mybir.dt.bfloat16
U8 = mybir.dt.uint8
AF = mybir.ActivationFunctionType
ALU = mybir.AluOpType

B, S, D = 2, 4096, 2048
H, HKV, DH = 16, 4, 128
T = 1024
W = 4096
PRED = 3072
CH = 128
NCO = T // CH
NCP = PRED // CH
EPS = 1e-5

ENGS = ("tensor", "vector", "scalar", "gpsimd", "sync")


class Em:

    def __init__(self, nc):
        self.nc = nc
        self.q = {e: [] for e in ENGS}
        self.sems = {}
        self.cnt = {}
        self.hw = {e: {} for e in ENGS}

    def sem(self, name):
        self.sems[name] = self.nc.alloc_semaphore(name=name)
        self.cnt[name] = 0

    def emit(self, eng, fn, inc=None, amt=1):
        if inc is not None:
            self.cnt[inc] += amt
            h = self.sems[inc]

            def wrapped(e, fn=fn, h=h, amt=amt):
                fn(e).then_inc(h, amt)

            self.q[eng].append(wrapped)
            return self.cnt[inc]
        self.q[eng].append(lambda e, fn=fn: fn(e))
        return None

    def wait(self, eng, sem, val=None):
        v = self.cnt[sem] if val is None else val
        if v <= self.hw[eng].get(sem, 0):
            return
        self.hw[eng][sem] = v
        h = self.sems[sem]
        self.q[eng].append(lambda e, h=h, v=v: e.wait_ge(h, v))

    def waitev(self, eng, ev):
        if ev is not None:
            self.wait(eng, ev[0], ev[1])

    def c(self, sem):
        return self.cnt[sem]


def _hilo(x64):
    hi = np.asarray(x64, dtype=np.float32)
    lo = np.asarray(x64 - hi.astype(np.float64), dtype=np.float32)
    return hi, lo


def _lam():
    i = np.arange(1, H + 1, dtype=np.float64)
    return np.exp2(-8.0 * i / H)


def _tables():
    lam = _lam()
    r = np.arange(CH, dtype=np.float64)
    diff = r[None, :] - r[:, None]
    diagT = np.where(diff >= 0, np.exp(-lam[:, None, None] * diff[None]), 0.0)
    diagT = diagT.transpose(1, 0, 2).reshape(CH, H * CH)
    qd = np.exp(-lam[:, None] * (r[None, :] + 1.0)).reshape(1, H * CH)
    qdec = np.broadcast_to(qd, (128, H * CH)).copy()
    kdec = np.exp(-lam[None, :] * (CH - 1.0 - r[:, None]))
    blk = np.broadcast_to(np.exp(-lam * CH)[None, :], (128, H)).copy()
    c = np.arange(NCP, dtype=np.float64)
    kd2 = np.exp(-lam[:, None, None] * (PRED - 1.0 - CH * c[None, :, None] - r[None, None, :]))
    kd2 = kd2.transpose(2, 0, 1).reshape(CH, H * NCP)
    return {"diagT": _hilo(diagT), "qdec": _hilo(qdec), "kdec": _hilo(kdec),
            "blk": _hilo(blk), "kd2": _hilo(kd2)}


def _rope_tables(r):
    inv = 1.0 / (10000.0 ** (np.arange(0, DH, 2, dtype=np.float64) / DH))
    pos = np.arange(W, dtype=np.float64) + (r + 1) * T - W
    valid = pos >= 0
    posc = np.where(valid, pos, 0.0)
    fr = posc[None, :] * inv[:, None]
    cos = np.where(valid[None, :], np.cos(fr), 1.0)
    sin = np.where(valid[None, :], np.sin(fr), 0.0)
    cosT = np.concatenate([cos, cos], axis=0)
    sinT = np.concatenate([sin, sin], axis=0)
    return _hilo(cosT), _hilo(sinT)


STOP = None
SKIP_TRANS = False


def build_nc(taps=()):
    nc = bass.Bass("TRN2", target_bir_lowering=False, debug=False, detect_race_conditions=False)

    def din(name, shape, dtype):
        return nc.dram_tensor(name, list(shape), dtype, kind="ExternalInput").ap()

    xw = din("xw", [2048, W], F32)
    xw2 = din("xw2", [6, 128, 16 * 512], F32)
    wk_h = din("wk_t", [HKV, 128, 16, 128], F32)
    wq_h = din("wq_t", [16, 128, 16, 128], F32)
    wo_h = din("wo_t", [16, 128, 16, 128], F32)
    wv_h = din("wv_t", [16, 128, 512], F32)
    gu_h = din("gu_t", [64, 128, 2, 16, 128], U8)
    dn_h = din("down_t", [16, 128, 64, 128], U8)
    gsc_h = din("gsc", [128, 64], F32)
    usc_h = din("usc", [128, 64], F32)
    dsc_h = din("dsc", [128, 16], F32)
    cos_hi = din("cos_hi", [128, W], F32)
    cos_lo = din("cos_lo", [128, W], F32)
    sin_hi = din("sin_hi", [128, W], F32)
    sin_lo = din("sin_lo", [128, W], F32)
    tab_h = {}
    for nm, cols in (("diagT", H * CH), ("qdec", H * CH), ("kdec", H), ("blk", H), ("kd2", H * NCP)):
        tab_h[nm] = (din(nm + "_hi", [128, cols], F32), din(nm + "_lo", [128, cols], F32))
    onesf_h = din("onesf", [128, 128], BF16)

    out_h = nc.dram_tensor("out", [2048, T], F32, kind="ExternalOutput").ap()
    tap_h = {}
    for tname, shape in (("q", [128, H * T]), ("k", [128, HKV * T]), ("vt", [128, NCO * 512]),
                         ("o", [128, H * T]), ("h1", [128, 16 * T]), ("r1", [128, 16 * T]),
                         ("kvin", [128, H * CH]), ("act", [128, 64 * 512]), ("s128", [128, 512])):
        if tname in taps:
            tap_h[tname] = nc.dram_tensor("tap_" + tname, shape, F32, kind="ExternalOutput").ap()

    gsc = nc.alloc_sbuf_tensor("gsc_s", [128, 64], F32).ap()
    usc = nc.alloc_sbuf_tensor("usc_s", [128, 64], F32).ap()
    dsc = nc.alloc_sbuf_tensor("dsc_s", [128, 16], F32).ap()
    onesf = nc.alloc_sbuf_tensor("onesf_s", [128, 128], BF16).ap()
    blkb = nc.alloc_sbuf_tensor("blk_s", [128, H], F32).ap()
    kdecb = nc.alloc_sbuf_tensor("kdec_s", [128, H], F32).ap()
    kd2b = nc.alloc_sbuf_tensor("kd2_s", [128, H * NCP], F32).ap()
    kvS_h = nc.alloc_sbuf_tensor("kvS", [128, H * 128], F32)
    kvS = kvS_h.ap()
    kvSb_h = nc.alloc_sbuf_tensor("kvSb", [128, H * 128], BF16)
    kvSb = kvSb_h.ap()

    base = (nc.sbuf_base + 31) & ~31
    off = [base]

    def at(name, shape, dtype, offset):
        return nc.alloc_sbuf_tensor_at(name, list(shape), dtype, offset=offset).ap()

    def region(sz):
        o = off[0]
        off[0] += sz
        return o

    R0 = region(16384)
    R1 = region(8192)
    R2 = region(24576)
    R3 = region(32768)
    R4 = region(65536)
    R5 = region(49152)
    assert off[0] <= 229344 - 64, f"sbuf overflow: {off[0]}"

    wvT = at("wvT", [128, 16 * 512], BF16, R0)
    h1fC = at("h1fC", [128, 2 * T], F32, R0)
    gub = at("gub", [128, 2 * 2 * 2048], BF16, R0 + 8192)
    wst = at("wst", [128, 2 * 16 * 128], BF16, R1)

    diagTb = at("diagTb", [128, H * CH], F32, R2)
    qdecb = at("qdecb", [128, H * CH], F32, R2 + 8192)
    cosO = at("cosO", [128, T], F32, R2 + 16384)
    sinO = at("sinO", [128, T], F32, R2 + 20480)
    h1fB = at("h1fB", [128, 6 * T], F32, R2)

    xb = at("xb", [128, 16 * T], BF16, R3)
    h1b = at("h1b", [128, 16 * T], BF16, R3)

    xpb = at("xpb", [128, 2 * 16 * 512], BF16, R4)
    ropeC = at("ropeC", [128, 2 * 512], F32, R4 + 32768)
    ropeS = at("ropeS", [128, 2 * 512], F32, R4 + 36864)
    ropeCl = at("ropeCl", [128, 2 * 512], F32, R4 + 40960)
    ropeSl = at("ropeSl", [128, 2 * 512], F32, R4 + 45056)
    ktp = at("ktp", [128, 4 * 512], BF16, R4 + 49152)
    vtp = at("vtp", [128, 4 * 512], BF16, R4 + 53248)
    rsc1 = at("rsc1", [128, 512], F32, R4 + 57344)
    rsc2 = at("rsc2", [128, 512], F32, R4 + 59392)
    kdtmp = at("kdtmp", [128, 2 * 128], BF16, R4 + 59392)
    lob = rsc1
    kroP = at("kroP", [128, 4 * 512], BF16, R4 + 61440)
    lob2 = at("lob2", [128, 512], F32, R4 + 53248)
    kro = at("kro", [128, HKV * T], BF16, R4)
    ktm = at("ktm", [128, NCO * 512], BF16, R4 + 8192)
    vtm = at("vtm", [128, NCO * 512], BF16, R4 + 16384)
    ofm = at("ofm", [128, H * T], BF16, R4 + 24576)
    xf = at("xf", [128, 2 * T], F32, R4 + 57344)
    rbt = at("rbt", [128, 2 * T], BF16, R4 + 57344)
    sqt = at("sqt", [128, 2 * T], BF16, R4 + 61440)
    act_s = at("act_s", [128, 64 * 512], BF16, R4)

    qro = at("qro", [128, H * T], BF16, R5)
    h1fA = at("h1fA", [128, 8 * T], F32, R5)
    wkT = at("wkT", [128, HKV * 16 * 128], BF16, R5 + 32768)
    s2a = at("s2a", [128, 8 * 128], BF16, R5 + 32768)
    qda = at("qda", [128, 8 * 128], BF16, R5 + 32768 + 2048)
    kdt2 = at("kdt2", [128, 8 * 128], BF16, R5 + 32768 + 4096)
    gsub = at("gsub", [128, 512], F32, R5 + 32768 + 6144)
    gact = at("gact", [128, 512], F32, R5 + 32768 + 8192)
    usub = at("usub", [128, 512], F32, R5 + 32768 + 10240)
    rbt2 = at("rbt2", [128, 512], BF16, R5 + 32768 + 12288)
    sqt2 = at("sqt2", [128, 512], BF16, R5 + 32768 + 13312)
    kdall = at("kdall", [128, 16 * 128], BF16, R1)
    kroP2 = at("kroP2", [128, 4 * 512], BF16, R1 + 4096)
    kvs_addr = nc.lookup_mloc(kvS_h).addr
    kvsb_addr = nc.lookup_mloc(kvSb_h).addr
    s128d = kvS[:, 0:512]
    s128b = at("s128b", [128, 512], F32, kvs_addr + 2048)
    dbr = at("dbr", [128, 16 * 128], BF16, kvs_addr + 4096)
    dbr2 = at("dbr2", [128, 16 * 128], BF16, kvsb_addr)

    def dbr_slot(g):
        return gub[:, (g % 4) * 2048:((g % 4) + 1) * 2048]

    def h1f(dp):
        if dp < 8:
            return h1fA[:, dp * T:(dp + 1) * T]
        if dp < 14:
            return h1fB[:, (dp - 8) * T:(dp - 7) * T]
        return h1fC[:, (dp - 14) * T:(dp - 13) * T]

    def act_ap(ic):
        return act_s[:, ic * 512:(ic + 1) * 512]

    pbig = [nc.alloc_psum_tensor(f"p{i}", [128, 512], F32).ap() for i in range(4)]
    psml = [nc.alloc_psum_tensor(f"q{i}", [128, 128], F32).ap() for i in range(4)]
    pA, pB, pC, pD = pbig

    em = Em(nc)
    for s in ("a", "a0", "a1", "gw", "gx0", "gx1", "gq0", "gq1", "gu0", "gu1", "gu2", "gu3",
              "gd0", "gd1", "gd2", "gd3", "st", "sx0", "sx1", "so", "pe", "dv", "ac", "po"):
        em.sem(s)

    bfree = {}

    def bank_wait(key):
        em.waitev("tensor", bfree.get(key))

    def bank_evac(key, sem):
        bfree[key] = (sem, em.c(sem))

    def dma(eng, sem, out, in_):
        em.emit(eng, lambda e, o=out, i=in_: e.dma_start(out=o, in_=i), inc=sem, amt=16)
        return (sem, em.c(sem))

    _tq = [0]

    def dmat(out, in_):
        if SKIP_TRANS:
            return ("st", em.c("st"))
        qe = "sync"
        em.emit(qe, lambda e, o=out, i=in_: e.dma_start_transpose(o, i), inc="st", amt=16)
        return ("st", em.c("st"))

    def mm(out, lhsT, rhs, start, stop, mark=False):
        v = em.emit("tensor",
                    lambda e, o=out, l=lhsT, r=rhs, st=start, sp=stop: e.matmul(o, l, r, start=st, stop=sp),
                    inc=("pe" if mark else None))
        return ("pe", v) if mark else None

    _ENG_SEM = {"vector": "dv", "scalar": "ac", "gpsimd": "po"}

    def tt(eng, out, in0, in1, op, inc=True):
        sem = _ENG_SEM[eng] if inc else None
        v = em.emit(eng, lambda e, o=out, a=in0, b=in1, p=op: e.tensor_tensor(o, a, b, p), inc=sem)
        return (sem, v) if inc else None

    def ts(eng, out, in0, scalar, op, inc=True):
        sem = _ENG_SEM[eng] if inc else None
        v = em.emit(eng, lambda e, o=out, a=in0, s=scalar, p=op: e.tensor_single_scalar(o, a, s, p), inc=sem)
        return (sem, v) if inc else None

    def stt(eng, out, in0, scalar, in1, op0, op1, inc=True):
        sem = _ENG_SEM[eng] if inc else None
        v = em.emit(eng, lambda e, o=out, a=in0, s=scalar, b=in1, p0=op0, p1=op1:
                    e.scalar_tensor_tensor(o, a, s, b, p0, p1), inc=sem)
        return (sem, v) if inc else None

    def cp(eng, out, in_, inc=True):
        sem = _ENG_SEM[eng] if inc else None
        if eng == "scalar":
            v = em.emit(eng, lambda e, o=out, i=in_: e.copy(o, i), inc=sem)
        else:
            v = em.emit(eng, lambda e, o=out, i=in_: e.tensor_copy(o, i), inc=sem)
        return (sem, v) if inc else None

    def act_fn(out, in_, func, scale=None):
        if scale is None:
            v = em.emit("scalar", lambda e, o=out, i=in_, f=func: e.activation(o, i, f), inc="ac")
        else:
            v = em.emit("scalar", lambda e, o=out, i=in_, f=func, s=scale: e.activation(o, i, f, scale=s), inc="ac")
        return ("ac", v)

    def rope_fm(psrc, cosA, sinA, out_bf):
        tt("vector", rsc1[:], psrc, cosA, ALU.mult, inc=False)
        tt("vector", rsc2[0:64, :], psrc[64:128, :], sinA[0:64, :], ALU.mult, inc=False)
        tt("vector", rsc2[64:128, :], psrc[0:64, :], sinA[64:128, :], ALU.mult, inc=False)
        tt("vector", out_bf[0:64, :], rsc1[0:64, :], rsc2[0:64, :], ALU.subtract, inc=False)
        return tt("vector", out_bf[64:128, :], rsc1[64:128, :], rsc2[64:128, :], ALU.add)

    for dst, src in ((gsc, gsc_h), (usc, usc_h), (dsc, dsc_h), (onesf, onesf_h)):
        dma("scalar", "a", dst, src)

    def combine(dst_full, hi_h, lo_h, cols):
        em.wait("scalar", "dv")
        dma("scalar", "a", dst_full, hi_h)
        for j0 in range(0, cols, 512):
            jw = min(512, cols - j0)
            em.wait("scalar", "dv")
            dma("scalar", "a", lob[:, 0:jw], lo_h[:, j0:j0 + jw])
            em.wait("vector", "a")
            tt("vector", dst_full[:, j0:j0 + jw], dst_full[:, j0:j0 + jw], lob[:, 0:jw], ALU.add)

    combine(kd2b, *tab_h["kd2"], H * NCP)
    combine(cosO, cos_hi[:, PRED:W], cos_lo[:, PRED:W], T)
    combine(sinO, sin_hi[:, PRED:W], sin_lo[:, PRED:W], T)
    combine(diagTb, *tab_h["diagT"], H * CH)
    combine(qdecb, *tab_h["qdec"], H * CH)
    combine(kdecb, *tab_h["kdec"], H)
    combine(blkb, *tab_h["blk"], H)

    for e in range(HKV):
        dma("gpsimd", "gw", wkT[:, e * 2048:(e + 1) * 2048], wk_h[e].rearrange("p d f -> p (d f)"))
    dma("gpsimd", "gw", wvT[:].rearrange("p (d f) -> p d f", d=16), wv_h.rearrange("d p f -> p d f"))
    dma("gpsimd", "gw", xb[:].rearrange("p (d t) -> p d t", d=16),
        xw[:, PRED:W].rearrange("(d p) t -> p d t", p=128))
    g_setup = ("gw", em.c("gw"))
    a_setup = ("a", em.c("a"))

    ev_kv0 = ("dv", em.emit("vector", lambda e: e.memset(kvS[:], 0.0), inc="dv"))

    pe_chunk_done = {}
    chunk_po_done = None
    chunk_pe_done = None
    ktp_prev1 = None
    ktp_prev2 = None
    u_cons = {}
    for pc in range(6):
        buf = pc % 2
        if pc >= 2:
            em.waitev("gpsimd", pe_chunk_done[pc - 2])
            em.waitev("scalar", pe_chunk_done[pc - 2])
        ev_x = dma("gpsimd", f"gx{buf}", xpb[:, buf * 8192:(buf + 1) * 8192], xw2[pc])
        dma("scalar", f"a{buf}", ropeC[:, buf * 512:(buf + 1) * 512], cos_hi[:, pc * 512:(pc + 1) * 512])
        dma("scalar", f"a{buf}", ropeCl[:, buf * 512:(buf + 1) * 512], cos_lo[:, pc * 512:(pc + 1) * 512])
        dma("scalar", f"a{buf}", ropeS[:, buf * 512:(buf + 1) * 512], sin_hi[:, pc * 512:(pc + 1) * 512])
        ev_sl = dma("scalar", f"a{buf}", ropeSl[:, buf * 512:(buf + 1) * 512], sin_lo[:, pc * 512:(pc + 1) * 512])
        em.waitev("vector", ev_sl)
        cosP = ropeC[:, buf * 512:(buf + 1) * 512]
        sinP = ropeS[:, buf * 512:(buf + 1) * 512]
        tt("vector", cosP, cosP, ropeCl[:, buf * 512:(buf + 1) * 512], ALU.add, inc=False)
        ev_rope = tt("vector", sinP, sinP, ropeSl[:, buf * 512:(buf + 1) * 512], ALU.add)

        xp = xpb[:, buf * 8192:(buf + 1) * 8192]
        if pc == 0:
            em.waitev("tensor", g_setup)
        em.waitev("tensor", ev_x)
        kroB = kroP if buf == 0 else kroP2
        ev_krop = {}
        for e in range(HKV):
            bank_wait(f"p{e}")
            for d in range(16):
                ev_mm = mm(pbig[e], wkT[:, (e * 16 + d) * 128:(e * 16 + d + 1) * 128],
                           xp[:, d * 512:(d + 1) * 512], d == 0, d == 15, mark=(d == 15))
            em.waitev("vector", ev_mm)
            em.waitev("vector", ev_rope)
            em.waitev("vector", ktp_prev2)
            ev_krop[e] = rope_fm(pbig[e], cosP, sinP, kroB[:, e * 512:(e + 1) * 512])
            bank_evac(f"p{e}", "dv")
        em.waitev("sync", chunk_pe_done)
        em.waitev("scalar", chunk_pe_done)
        ev_ktp = {}
        for tsp in range(4):
            for e in range(HKV):
                em.waitev("sync", ev_krop[e])
                em.waitev("scalar", ev_krop[e])
                ev_ktp[tsp] = dmat(ktp[:, (tsp * 4 + e) * 128:(tsp * 4 + e + 1) * 128],
                                   kroB[:, e * 512 + tsp * 128: e * 512 + (tsp + 1) * 128])
        ktp_prev2 = ktp_prev1
        ktp_prev1 = ev_ktp[3]
        ev_vtp = {}
        for tts in range(4):
            bank_wait(f"p{tts}")
            for d in range(16):
                ev_mm = mm(pbig[tts], xp[:, d * 512 + tts * 128: d * 512 + (tts + 1) * 128],
                           wvT[:, d * 512:(d + 1) * 512], d == 0, d == 15, mark=(d == 15))
            em.waitev("scalar", ev_mm)
            em.waitev("scalar", chunk_pe_done)
            ev_vtp[tts] = cp("scalar", vtp[:, tts * 512:(tts + 1) * 512], pbig[tts])
            bank_evac(f"p{tts}", "ac")
        ev_kb = {}
        u_prev = u_cons
        u_cons = {}
        for m in range(64 + 16):
            mc = m - 16
            if 0 <= mc < 64:
                hh, sc = mc // 4, mc % 4
                pu = psml[hh % 4]
                em.waitev("tensor", ev_kb[mc])
                em.waitev("tensor", ev_vtp[sc])
                if sc == 0:
                    bank_wait(f"q{hh % 4}")
                u_cons[mc] = mm(pu, kdall[:, (mc % 16) * 128:(mc % 16 + 1) * 128],
                                vtp[:, sc * 512 + (hh // 4) * 128: sc * 512 + (hh // 4) * 128 + 128],
                                sc == 0, sc == 3, mark=True)
                if sc == 3:
                    em.waitev("vector", u_cons[mc])
                    em.waitev("vector", ev_kv0)
                    tt("vector", kvS[:, hh * 128:(hh + 1) * 128],
                       kvS[:, hh * 128:(hh + 1) * 128], pu, ALU.add)
                    bank_evac(f"q{hh % 4}", "dv")
            if m < 64:
                hh, sc = m // 4, m % 4
                em.waitev("vector", ev_ktp[3])
                em.waitev("vector", u_cons.get(m - 16) if m >= 16 else u_prev.get(m + 48))
                ev_kb[m] = ts("vector", kdall[:, (m % 16) * 128:(m % 16 + 1) * 128],
                              ktp[:, (sc * 4 + (hh // 4)) * 128:(sc * 4 + (hh // 4) + 1) * 128],
                              kd2b[:, hh * NCP + (pc * 4 + sc): hh * NCP + (pc * 4 + sc) + 1], ALU.mult)
        pe_chunk_done[pc] = ("pe", em.c("pe"))
        chunk_pe_done = pe_chunk_done[pc]
        chunk_po_done = ("po", em.c("po"))
    ev_kvin = ("dv", em.c("dv"))

    if "kvin" in tap_h:
        em.waitev("sync", ev_kvin)
        dma("sync", "so", tap_h["kvin"], kvS[:])

    def finish():
        em.emit("vector", lambda e: e.memset(h1fA[:, 0:T], 0.0), inc="dv")
        em.waitev("sync", ("dv", em.c("dv")))
        em.waitev("sync", ("ac", em.c("ac")))
        em.waitev("sync", ("po", em.c("po")))
        dma("sync", "so", out_h[0:128, :], h1fA[:, 0:T])
        em.wait("sync", "so", em.c("so"))
        with nc.Block() as block:
            for eng in ENGS:
                ops = em.q[eng]

                def run(e, ops=ops):
                    for op in ops:
                        op(e)

                getattr(block, eng)(run)
        return nc, None

    if STOP == 1:
        return finish()

    em.waitev("tensor", g_setup)
    ev_krop_own = {}
    for e in range(HKV):
        for tq in range(2):
            g_i = e * 2 + tq
            bank_wait(f"p{g_i % 4}")
            for d in range(16):
                ev_mm = mm(pbig[g_i % 4], wkT[:, (e * 16 + d) * 128:(e * 16 + d + 1) * 128],
                           xb[:, d * T + tq * 512: d * T + (tq + 1) * 512], d == 0, d == 15, mark=(d == 15))
            em.waitev("vector", ev_mm)
            ev_krop_own[(e, tq)] = rope_fm(pbig[g_i % 4], cosO[:, tq * 512:(tq + 1) * 512],
                                           sinO[:, tq * 512:(tq + 1) * 512],
                                           kro[:, e * T + tq * 512: e * T + (tq + 1) * 512])
            bank_evac(f"p{g_i % 4}", "dv")
    em.waitev("sync", chunk_pe_done)
    em.waitev("scalar", chunk_pe_done)
    ev_ktm = None
    for c in range(NCO):
        for e in range(HKV):
            em.waitev("sync", ev_krop_own[(e, c // 4)])
            em.waitev("scalar", ev_krop_own[(e, c // 4)])
            ev_ktm = dmat(ktm[:, c * 512 + e * 128: c * 512 + (e + 1) * 128],
                          kro[:, e * T + c * 128: e * T + (c + 1) * 128])
    ev_vtm_all = None
    for tts in range(NCO):
        bank_wait(f"p{tts % 4}")
        for d in range(16):
            ev_mm = mm(pbig[tts % 4], xb[:, d * T + tts * 128: d * T + (tts + 1) * 128],
                       wvT[:, d * 512:(d + 1) * 512], d == 0, d == 15, mark=(d == 15))
        em.waitev("scalar", ev_mm)
        em.waitev("scalar", chunk_pe_done)
        ev_vtm_all = cp("scalar", vtm[:, tts * 512:(tts + 1) * 512], pbig[tts % 4])
        bank_evac(f"p{tts % 4}", "ac")
    ev_q_done = {}
    for e in range(16):
        wb = wst[:, (e % 2) * 2048:(e % 2 + 1) * 2048]
        if e >= 2:
            em.waitev("gpsimd", ev_q_done[e - 2])
        else:
            em.waitev("gpsimd", pe_chunk_done[5])
        ev_w = dma("gpsimd", f"gq{e % 2}", wb, wq_h[e].rearrange("p d f -> p (d f)"))
        em.waitev("tensor", ev_w)
        for tq in range(2):
            g_i = e * 2 + tq
            bank_wait(f"p{g_i % 4}")
            for d in range(16):
                ev_mm = mm(pbig[g_i % 4], wb[:, d * 128:(d + 1) * 128],
                           xb[:, d * T + tq * 512: d * T + (tq + 1) * 512], d == 0, d == 15, mark=(d == 15))
            ev_q_done[e] = ev_mm
            em.waitev("vector", ev_mm)
            rope_fm(pbig[g_i % 4], cosO[:, tq * 512:(tq + 1) * 512], sinO[:, tq * 512:(tq + 1) * 512],
                    qro[:, e * T + tq * 512: e * T + (tq + 1) * 512])
            bank_evac(f"p{g_i % 4}", "dv")
    ev_own = ("dv", em.c("dv"))

    if "q" in tap_h:
        em.waitev("gpsimd", ev_own)
        dma("gpsimd", "so", tap_h["q"], qro[:])
    if "k" in tap_h:
        em.waitev("gpsimd", ev_own)
        dma("gpsimd", "so", tap_h["k"], kro[:])
    if "vt" in tap_h:
        em.waitev("gpsimd", ev_vtm_all)
        dma("gpsimd", "so", tap_h["vt"], vtm[:])

    if STOP == 2:
        return finish()

    em.waitev("vector", ev_kvin)
    ev_kvsb = cp("vector", kvSb[:], kvS[:])

    its = [(c, hh) for c in range(NCO) for hh in range(H)]
    ev_s2, ev_qd, ev_kb2, ev_omm, ev_umm, ev_stt, ev_cast = {}, {}, {}, {}, {}, {}, {}

    def scan_front(i):
        c, hh = its[i]
        e = hh // 4
        tog = i % 8
        if i >= 8:
            em.waitev("vector", ev_omm.get(i - 8))
            em.waitev("scalar", ev_umm.get(i - 8))
        em.waitev("scalar", ev_ktm)
        if i == 0:
            em.waitev("scalar", ev_kv0)
        ev_qd[i] = tt("vector", qda[:, tog * 128:(tog + 1) * 128],
                      qro[:, hh * T + c * 128:hh * T + (c + 1) * 128],
                      qdecb[:, hh * 128:(hh + 1) * 128], ALU.mult)
        ev_kb2[i] = ("ac", em.emit("scalar", lambda e_, o=kdt2[:, tog * 128:(tog + 1) * 128],
                                   a=ktm[:, c * 512 + e * 128: c * 512 + (e + 1) * 128],
                                   m=kdecb[:, hh:hh + 1]: e_.mul(o, a, m), inc="ac"))
        pS = psml[i % 2]
        bank_wait(f"q{i % 2}")
        if i == 0:
            em.waitev("tensor", ev_ktm)
        ev = mm(pS, kro[:, e * T + c * 128:e * T + (c + 1) * 128],
                qro[:, hh * T + c * 128:hh * T + (c + 1) * 128], True, True, mark=True)
        em.waitev("vector", ev)
        ev_s2[i] = tt("vector", s2a[:, tog * 128:(tog + 1) * 128], pS,
                      diagTb[:, hh * 128:(hh + 1) * 128], ALU.mult)
        bfree[f"q{i % 2}"] = ev_s2[i]

    def scan_back(i):
        c, hh = its[i]
        e = hh // 4
        tog = i % 8
        pO = psml[2 + i % 2]
        pS = psml[i % 2]
        bank_wait(f"q{2 + i % 2}")
        em.waitev("tensor", ev_s2[i])
        em.waitev("tensor", ev_qd[i])
        em.waitev("tensor", ev_cast.get(i - H, ev_kvsb))
        mm(pO, vtm[:, c * 512 + e * 128: c * 512 + (e + 1) * 128],
           s2a[:, tog * 128:(tog + 1) * 128], True, False)
        ev_omm[i] = mm(pO, kvSb[:, hh * 128:(hh + 1) * 128],
                       qda[:, tog * 128:(tog + 1) * 128], False, True, mark=True)
        em.waitev("tensor", ev_kb2[i])
        ev_umm[i] = mm(pS, kdt2[:, tog * 128:(tog + 1) * 128],
                       vtm[:, c * 512 + e * 128: c * 512 + (e + 1) * 128], True, True, mark=True)
        em.waitev("scalar", ev_omm[i])
        cp("scalar", ofm[:, hh * T + c * 128: hh * T + (c + 1) * 128], pO)
        bank_evac(f"q{2 + i % 2}", "ac")
        em.waitev("vector", ev_umm[i])
        ev_stt[i] = stt("vector", kvS[:, hh * 128:(hh + 1) * 128], kvS[:, hh * 128:(hh + 1) * 128],
                        blkb[:, hh:hh + 1], pS, ALU.mult, ALU.add)
        bfree[f"q{i % 2}"] = ev_stt[i]
        em.waitev("scalar", ev_stt[i])
        ev_cast[i] = cp("scalar", kvSb[:, hh * 128:(hh + 1) * 128], kvS[:, hh * 128:(hh + 1) * 128])

    for i in range(len(its)):
        scan_front(i)
        if i > 0:
            scan_back(i - 1)
    scan_back(len(its) - 1)
    ev_scan = ("ac", em.c("ac"))

    if "o" in tap_h:
        em.waitev("gpsimd", ev_scan)
        dma("gpsimd", "so", tap_h["o"], ofm[:])

    if STOP == 3:
        return finish()

    ev_op_done = {}
    ev_xf_used = {}
    for dp in range(16):
        wb = wst[:, (dp % 2) * 2048:(dp % 2 + 1) * 2048]
        if dp < 2:
            em.waitev("gpsimd", ("pe", em.c("pe")))
            em.waitev("sync", ev_scan)
        else:
            em.waitev("gpsimd", ev_op_done[dp - 2])
            em.waitev("sync", ev_xf_used[dp - 2])
        ev_w = dma("gpsimd", f"gq{dp % 2}", wb, wo_h[dp].rearrange("p d f -> p (d f)"))
        ev_xf = dma("sync", f"sx{dp % 2}", xf[:, (dp % 2) * T:(dp % 2 + 1) * T],
                    xw[:, PRED:W].rearrange("(d p) t -> d p t", p=128)[dp])
        em.waitev("tensor", ev_w)
        if dp == 0:
            em.waitev("tensor", ev_scan)
        for tq in range(2):
            bk = (dp * 2 + tq) % 4
            bank_wait(f"p{bk}")
            for e in range(16):
                ev_mm = mm(pbig[bk], wb[:, e * 128:(e + 1) * 128],
                           ofm[:, e * T + tq * 512: e * T + (tq + 1) * 512], e == 0, e == 15, mark=(e == 15))
            ev_op_done[dp] = ev_mm
            em.waitev("vector", ev_mm)
            em.waitev("vector", ev_xf)
            ev = tt("vector", h1f(dp)[:, tq * 512:(tq + 1) * 512], pbig[bk],
                    xf[:, (dp % 2) * T + tq * 512:(dp % 2) * T + (tq + 1) * 512], ALU.add)
            bank_evac(f"p{bk}", "dv")
        ev_xf_used[dp] = ev

    if "r1" in tap_h:
        em.waitev("sync", ("dv", em.c("dv")))
        for dp in range(16):
            dma("sync", "so", tap_h["r1"][:, dp * T:(dp + 1) * T], h1f(dp))
        em.wait("vector", "so")

    def layer_norm(get_tile, tq_count, rcast, scast, nbuf, ready_dv, ready_po=None, post_tq=None,
                   post_dp=None):
        stat_m = (pC, pD)
        stat_s = (pA, pB)
        em.waitev("scalar", ready_dv)
        em.waitev("vector", ready_dv)
        if ready_po is not None:
            em.waitev("scalar", ready_po)
            em.waitev("vector", ready_po)
        ev_stats = {}
        for dp in range(16):
            w = tq_count * 512
            rb = rcast[:, (dp % nbuf) * w:((dp % nbuf) + 1) * w]
            sq = scast[:, (dp % nbuf) * w:((dp % nbuf) + 1) * w]
            if dp >= nbuf:
                em.waitev("scalar", ev_stats[dp - nbuf])
                em.waitev("vector", ev_stats[dp - nbuf])
            ev_rb = cp("scalar", rb, get_tile(dp))
            ev_sq = tt("vector", sq, get_tile(dp), get_tile(dp), ALU.mult)
            em.waitev("tensor", ev_rb)
            em.waitev("tensor", ev_sq)
            if dp == 0:
                for tq in range(tq_count):
                    bank_wait(f"p{2 + tq}")
                    bank_wait(f"p{0 + tq}")
            for tq in range(tq_count):
                mm(stat_m[tq], onesf, rb[:, tq * 512:(tq + 1) * 512], dp == 0, dp == 15)
                ev_stats[dp] = mm(stat_s[tq], onesf, sq[:, tq * 512:(tq + 1) * 512],
                                  dp == 0, dp == 15, mark=(tq == tq_count - 1))
        ev_ap_d = None
        ev_ap_p = None
        for tq in range(tq_count):
            em.waitev("vector", ev_stats[15])
            ts("vector", gsub[:], stat_m[tq], 1.0 / 2048.0, ALU.mult, inc=False)
            ts("vector", gact[:], stat_s[tq], 1.0 / 2048.0, ALU.mult, inc=False)
            tt("vector", usub[:], gsub[:], gsub[:], ALU.mult, inc=False)
            tt("vector", gact[:], gact[:], usub[:], ALU.subtract, inc=False)
            ev = ts("vector", gact[:], gact[:], EPS, ALU.add)
            bank_evac(f"p{2 + tq}", "dv")
            bank_evac(f"p{0 + tq}", "dv")
            em.waitev("scalar", ev)
            act_fn(rsc2[:], gact[:], AF.Ln)
            ev = act_fn(usub[:], rsc2[:], AF.Exp, scale=-0.5)
            em.waitev("vector", ev)
            tt("vector", rsc2[:], usub[:], usub[:], ALU.mult, inc=False)
            tt("vector", rsc2[:], rsc2[:], gact[:], ALU.mult, inc=False)
            em.emit("vector", lambda e, o=rsc2: e.tensor_scalar(
                o[:], o[:], -0.5, 1.5, ALU.mult, ALU.add))
            tt("vector", gact[:], usub[:], rsc2[:], ALU.mult, inc=False)
            for dp in range(16):
                tile = get_tile(dp)[:, tq * 512:(tq + 1) * 512]
                tt("vector", rsc1[:], tile, gsub[:], ALU.subtract, inc=False)
                ev_ap_d = tt("vector", tile, rsc1[:], gact[:], ALU.mult)
                if post_dp is not None:
                    post_dp(dp, ev_ap_d)
            if post_tq is not None:
                post_tq(tq)
        return ev_ap_d, ev_ap_d

    em.waitev("tensor", a_setup)
    em.waitev("scalar", a_setup)
    em.waitev("vector", a_setup)
    ev_r1 = ("dv", em.c("dv"))
    ev_h1b_half = {}

    def _ln1_post(tq):
        for dp in range(16):
            cp("vector", h1b[:, dp * T + tq * 512: dp * T + (tq + 1) * 512],
               h1f(dp)[:, tq * 512:(tq + 1) * 512], inc=(dp == 15))
        ev_h1b_half[tq] = ("dv", em.c("dv"))

    layer_norm(lambda dp: h1f(dp), 2, rbt, sqt, 2, ev_r1, post_tq=_ln1_post)
    ev_h1b = ev_h1b_half[1]
    ev_h1b_d = ev_h1b_half[1]

    if "h1" in tap_h:
        em.waitev("sync", ev_h1b)
        em.waitev("sync", ev_h1b_d)
        for dp in range(16):
            dma("sync", "so", tap_h["h1"][:, dp * T:(dp + 1) * T], h1f(dp))

    if STOP == 4:
        return finish()

    ev_gu = {}
    ev_gg = {}
    ev_act = {}
    for th in range(2):
        em.waitev("tensor", ev_h1b_half[th])
        bank_wait("p2")
        for d in range(16):
            ev_mm = mm(pC, onesf, h1b[:, d * T + th * 512: d * T + (th + 1) * 512],
                       d == 0, d == 15, mark=(d == 15))
        em.waitev("vector", ev_mm)
        ev_s128 = ts("vector", s128b[:], pC, 128.0, ALU.mult)
        bank_evac("p2", "dv")

        for i in range(64):
            gb = gub[:, (i % 2) * 2048:(i % 2 + 1) * 2048]
            ub = gub[:, 4096 + (i % 2) * 2048: 4096 + (i % 2 + 1) * 2048]
            if (th, i) == (0, 0):
                em.waitev("gpsimd", ev_op_done[15])
            else:
                prevg = ev_gg.get((th, i - 2)) if i >= 2 else ev_gg.get((th - 1, 62 + i))
                if prevg is not None:
                    em.waitev("gpsimd", prevg)
                if th == 1 and i == 0:
                    em.waitev("gpsimd", prev_th_grp)
            ev_wg = dma("gpsimd", f"gu{i % 2}", gb, gu_h[i].rearrange("p g h f -> g p (h f)")[0])
            if (th, i) != (0, 0):
                prevu = ev_gu.get((th, i - 2)) if i >= 2 else ev_gu.get((th - 1, 62 + i))
                if prevu is not None:
                    em.waitev("gpsimd", prevu)
            ev_wu = dma("gpsimd", f"gu{2 + i % 2}", ub, gu_h[i].rearrange("p g h f -> g p (h f)")[1])
            em.waitev("tensor", ev_wg)
            if i == 0:
                em.waitev("tensor", ev_s128)
            bg, bu = (0, 1) if i % 2 == 0 else (2, 3)
            bank_wait(f"p{bg}")
            for hc in range(16):
                ev_g = mm(pbig[bg], gb[:, hc * 128:(hc + 1) * 128],
                          h1b[:, hc * T + th * 512: hc * T + (th + 1) * 512], hc == 0, hc == 15,
                          mark=(hc == 15))
            ev_gg[(th, i)] = ev_g
            em.waitev("tensor", ev_wu)
            bank_wait(f"p{bu}")
            for hc in range(16):
                ev_u = mm(pbig[bu], ub[:, hc * 128:(hc + 1) * 128],
                          h1b[:, hc * T + th * 512: hc * T + (th + 1) * 512], hc == 0, hc == 15,
                          mark=(hc == 15))
            ev_gu[(th, i)] = ev_u
            em.waitev("vector", ev_g)
            em.waitev("vector", ev_s128)
            ev = tt("vector", gsub[:], pbig[bg], s128b[:], ALU.subtract)
            bank_evac(f"p{bg}", "dv")
            em.waitev("scalar", ev)
            ev_si = act_fn(gact[:], gsub[:], AF.Silu, scale=gsc[:, i:i + 1])
            em.waitev("vector", ev_u)
            ev = tt("vector", usub[:], pbig[bu], s128b[:], ALU.subtract)
            bank_evac(f"p{bu}", "dv")
            em.waitev("vector", ev_si)
            ev_act[(th, i)] = stt("vector", act_ap(i), usub[:], usc[:, i:i + 1], gact[:],
                                  ALU.mult, ALU.mult)

        bank_wait("p0")
        em.waitev("tensor", ev_act[(th, 63)])
        for ic in range(64):
            ev_mm = mm(pA, onesf, act_ap(ic), ic == 0, ic == 63, mark=(ic == 63))
        em.waitev("vector", ev_mm)
        ev_s128d = ts("vector", s128d, pA, 128.0, ALU.mult)
        bank_evac("p0", "dv")

        ev_dn = {}
        ev_grp = {}

        def dn_dma(g):
            hp2, grp2 = g // 4, g % 4
            if g >= 4:
                em.waitev("gpsimd", ev_grp.get(g - 4))
            elif g == 0:
                em.waitev("gpsimd", ev_gg[(th, 63)])
                em.waitev("gpsimd", ev_gu[(th, 63)])
            ev_dn[g] = dma("gpsimd", f"gd{g % 4}",
                           dbr_slot(g),
                           dn_h[hp2].rearrange("p c f -> p (c f)")[:, grp2 * 2048:(grp2 + 1) * 2048])

        dn_dma(0)
        dn_dma(1)
        dn_dma(2)
        dn_dma(3)
        for hp in range(16):
            bk = 2 + hp % 2
            bank_wait(f"p{bk}")
            for grp in range(4):
                g = hp * 4 + grp
                em.waitev("tensor", ev_dn[g])
                if hp == 0 and grp == 0:
                    em.waitev("tensor", ev_s128d)
                for j in range(16):
                    ic = grp * 16 + j
                    ev_grp[g] = mm(pbig[bk], dbr_slot(g)[:, j * 128:(j + 1) * 128],
                                   act_ap(ic), ic == 0, ic == 63, mark=(j == 15))
                if g + 4 < 64:
                    dn_dma(g + 4)
            em.waitev("vector", ev_grp[hp * 4 + 3])
            em.waitev("vector", ev_s128d)
            tt("vector", gsub[:], pbig[bk], s128d, ALU.subtract, inc=False)
            stt("vector", h1f(hp)[:, th * 512:(th + 1) * 512], gsub[:], dsc[:, hp:hp + 1],
                h1f(hp)[:, th * 512:(th + 1) * 512], ALU.mult, ALU.add)
            bank_evac(f"p{bk}", "dv")
        prev_th_grp = ev_grp[63]

        if th == 0 and "act" in tap_h:
            em.waitev("gpsimd", ev_act[(0, 63)])
            dma("gpsimd", "so", tap_h["act"], act_s[:])
        if th == 0 and "s128" in tap_h:
            em.waitev("sync", ev_s128)
            dma("sync", "so", tap_h["s128"], s128b[:])

        def get2(dp, th=th):
            return h1f(dp)[:, th * 512:(th + 1) * 512]

        ev_r2 = ("dv", em.c("dv"))

        def _store_dp(dp, ev, th=th):
            em.waitev("sync", ev)
            dma("sync", "so",
                out_h.rearrange("(d p) t -> d p t", p=128)[dp][:, th * 512:(th + 1) * 512],
                h1f(dp)[:, th * 512:(th + 1) * 512])

        layer_norm(get2, 1, rbt2, sqt2, 1, ev_r2, post_dp=_store_dp)

    em.wait("sync", "so", em.c("so"))

    with nc.Block() as block:
        for eng in ENGS:
            ops = em.q[eng]

            def run(e, ops=ops):
                for op in ops:
                    op(e)

            getattr(block, eng)(run)

    return nc, None



def _prep_shared(wq, wk, wv, wo, gw, uw, dw, gs, us, ds):
    def f32(a):
        return np.ascontiguousarray(np.asarray(a, dtype=np.float32))

    wq, wk, wv, wo = f32(wq), f32(wk), f32(wv), f32(wo)

    def u8(a):
        a = np.ascontiguousarray(np.asarray(a))
        assert a.dtype == np.int32
        return np.ascontiguousarray(a.view(np.uint8)[:, 0::4])

    g8, u8w, d8 = u8(gw), u8(uw), u8(dw)

    d = {}
    d["wq_t"] = np.ascontiguousarray(wq.reshape(16, 128, 16, 128).transpose(0, 3, 2, 1))
    d["wk_t"] = np.ascontiguousarray(wk.reshape(HKV, 128, 16, 128).transpose(0, 3, 2, 1))
    d["wo_t"] = np.ascontiguousarray(wo.reshape(16, 128, 16, 128).transpose(0, 3, 2, 1))
    d["wv_t"] = np.ascontiguousarray(wv.reshape(HKV * 128, 16, 128).transpose(1, 2, 0))
    gt = g8.reshape(64, 128, 16, 128).transpose(0, 3, 2, 1)
    ut = u8w.reshape(64, 128, 16, 128).transpose(0, 3, 2, 1)
    d["gu_t"] = np.ascontiguousarray(np.stack([gt, ut], axis=1).transpose(0, 2, 1, 3, 4))
    d["down_t"] = np.ascontiguousarray(d8.reshape(16, 128, 64, 128).transpose(0, 3, 2, 1))
    d["gsc"] = np.ascontiguousarray(np.asarray(gs, np.float32).reshape(64, 128).T)
    d["usc"] = np.ascontiguousarray(np.asarray(us, np.float32).reshape(64, 128).T)
    d["dsc"] = np.ascontiguousarray(np.asarray(ds, np.float32).reshape(16, 128).T)
    t = _tables()
    for nm in ("diagT", "qdec", "kdec", "blk", "kd2"):
        d[nm + "_hi"], d[nm + "_lo"] = t[nm]
    d["onesf"] = np.ones((128, 128), dtype=ml_dtypes.bfloat16)
    return d


def _prep_core(x, b, r):
    xT = np.asarray(x[b], dtype=np.float32).T
    winw = np.zeros((2048, W), dtype=np.float32)
    start = (r + 1) * T - W
    if start < 0:
        winw[:, -start:] = xT[:, 0:(r + 1) * T]
    else:
        winw[:, :] = xT[:, start:(r + 1) * T]
    (chi, clo), (shi, slo) = _rope_tables(r)
    xw2 = np.ascontiguousarray(
        winw[:, :PRED].reshape(16, 128, 6, 512).transpose(2, 1, 0, 3).reshape(6, 128, 16 * 512))
    return {"xw": np.ascontiguousarray(winw), "xw2": xw2,
            "cos_hi": np.ascontiguousarray(chi), "cos_lo": np.ascontiguousarray(clo),
            "sin_hi": np.ascontiguousarray(shi), "sin_lo": np.ascontiguousarray(slo)}


def kernel(L0_attn_x, L0_attn_wq, L0_attn_wk, L0_attn_wv, L0_attn_wo,
           L0_ffn_gate_w, L0_ffn_gate_scale, L0_ffn_up_w, L0_ffn_up_scale,
           L0_ffn_down_w, L0_ffn_down_scale):
    shared = _prep_shared(L0_attn_wq, L0_attn_wk, L0_attn_wv, L0_attn_wo,
                          L0_ffn_gate_w, L0_ffn_up_w, L0_ffn_down_w,
                          L0_ffn_gate_scale, L0_ffn_up_scale, L0_ffn_down_scale)
    x = np.asarray(L0_attn_x, dtype=np.float32)
    in_maps = []
    for c in range(8):
        m = dict(shared)
        m.update(_prep_core(x, c // 4, c % 4))
        in_maps.append(m)
    nc, _ = build_nc()
    res = run_bass_kernel_spmd(nc, in_maps, core_ids=list(range(8)))
    full = np.empty((B, D, S), dtype=np.float32)
    for c in range(8):
        b, r = c // 4, c % 4
        full[b, :, r * T:(r + 1) * T] = res.results[c]["out"]
    return np.ascontiguousarray(full.transpose(0, 2, 1))
